# version 10 (final)
# speedup vs baseline: 1.3014x; 1.0005x over previous
import numpy as np
import ml_dtypes
from contextlib import ExitStack

import concourse.bass as bass
import concourse.mybir as mybir
from concourse.bass_utils import run_bass_kernel_spmd
from concourse.dve_op import DveOp
from concourse.dve_spec import (
    Spec, Src0, Src1, C0, C1, C2, Zero, One, sq, select, lower as dve_lower,
)
from concourse import dve_ops as _dvo

F32 = mybir.dt.float32
BF16 = mybir.dt.bfloat16
AF = mybir.ActivationFunctionType
ALU = mybir.AluOpType
AX = mybir.AxisListType.X

NCORES = 8
D = 1024
DI = 2048
NH, HD, DS = 32, 64, 128
CD = DI + 2 * DS
DIP = 2 * DI + 2 * DS + NH
FF = 2816
NB_D = D // 16
NB_F = FF // 16
EPS = 1e-5

FRAME = 2176
NT = FRAME // 128
TREAL = 2048
NCH = 16
O_CHUNKS = [(0, 512), (512, 512), (1024, 512), (1536, 512), (2048, 512),
            (2560, 256)]
NSC = 4
ARENA_W = 47640

_REGISTERED = {}


def _register_op(name, body, ref, victim):
    if name in _REGISTERED:
        return _REGISTERED[name]
    sp = Spec(body=body)
    sha = dve_lower(sp, ver="v3", perf_en=False, name=name).sha("v3")
    op = DveOp(name, sp, subdim=False, uops_sha={"v3": sha}, reference=ref)
    i = next(j for j, o in enumerate(_dvo.OPS) if o.name == victim)
    _dvo.OPS[i] = op
    _dvo._SUB_OPCODE_FOR_NAME[op.name] = i + 1
    _dvo.CUSTOM_DVE_SPECS[op.name] = sp
    _dvo.CUSTOM_DVE_REFS[op.name] = ref
    _REGISTERED[name] = op
    return op


_s2 = sq(Src0)
QSNAP_A = _register_op(
    "QSNAP_A",
    (_s2 > C0) + (_s2 > C1) + (_s2 > C2),
    lambda in0, in1, s0, s1, imm2: (
        (in0 * in0 > s0).astype(in0.dtype) + (in0 * in0 > s1)
        + (in0 * in0 > imm2)).astype(in0.dtype),
    "DVEHASH32L_SEED",
)
_b2 = (_s2 > C1) + (_s2 > C2)
QSNAP_B = _register_op(
    "QSNAP_B",
    (Src1 + (_s2 > C0)) + (_b2 + _b2),
    lambda in0, in1, s0, s1, imm2: (
        in1 + (in0 * in0 > s0)
        + 2.0 * ((in0 * in0 > s1).astype(in0.dtype) + (in0 * in0 > imm2))
    ).astype(in0.dtype),
    "DVEHASH32L_BODY",
)
_g = Src0 >= Zero
QSNAP_C = _register_op(
    "QSNAP_C",
    select(_s2 > C0, C1, Src1) * ((_g + _g) - One),
    lambda in0, in1, s0, s1, imm2: (
        np.where(in0 * in0 > s0, s1, in1) * (2.0 * (in0 >= 0) - 1.0)
    ).astype(in0.dtype),
    "DVEHASH32L_DRAIN",
)


class Ctr:
    __slots__ = ("sem", "n")

    def __init__(self, sem):
        self.sem, self.n = sem, 0


class Prog:
    ENGS = ("sync", "scalar", "vector", "tensor", "gpsimd")

    def __init__(self):
        self.ops = {e: [] for e in self.ENGS}

    def emit(self, eng, fn):
        self.ops[eng].append(fn)

    def wait(self, eng, ctr, val=None):
        v = ctr.n if val is None else val
        if v <= 0:
            return
        self.emit(eng, lambda e, s=ctr.sem, vv=v: e.wait_ge(s, vv))

    def op(self, eng, build, inc=None, k=1):
        if inc is not None:
            inc.n += k

            def f(e, b=build, s=inc.sem, kk=k):
                b(e).then_inc(s, kk)
        else:
            def f(e, b=build):
                b(e)
        self.emit(eng, f)
        return inc.n if inc is not None else None

    def run(self, block):
        for eng in self.ENGS:
            deco = getattr(block, eng)

            def body(e, _eng=eng):
                for f in self.ops[_eng]:
                    f(e)
            deco(body)


DBG_TAPS = []
DBG_OUT = {}


def _np_slice_inputs(inputs):
    x = np.ascontiguousarray(np.asarray(inputs["L0_attn_x"], np.float32)
                             .reshape(16384, D))
    wg = np.ascontiguousarray(np.asarray(inputs["L0_ffn_w_gate"], np.float32))
    wu = np.ascontiguousarray(np.asarray(inputs["L0_ffn_w_up"], np.float32))
    wd = np.ascontiguousarray(np.asarray(inputs["L0_ffn_w_down"], np.float32))
    wipT = np.ascontiguousarray(
        np.asarray(inputs["L0_attn_in_proj_w"], np.float32).T)
    woutT = np.ascontiguousarray(
        np.asarray(inputs["L0_attn_out_proj_w"], np.float32).T)
    lnw = np.ascontiguousarray(
        np.asarray(inputs["L0_attn_ln_w"], np.float32).reshape(8, 128).T)
    lnb = np.ascontiguousarray(
        np.asarray(inputs["L0_attn_ln_b"], np.float32).reshape(8, 128).T)
    convw = np.ascontiguousarray(
        np.asarray(inputs["L0_attn_conv_w"], np.float32)
        .reshape(18, 128, 4).transpose(1, 0, 2).reshape(128, 72))
    convb = np.ascontiguousarray(
        np.asarray(inputs["L0_attn_conv_b"], np.float32)
        .reshape(18, 128).T)
    alog = np.ascontiguousarray(
        np.asarray(inputs["L0_attn_A_log"], np.float32).reshape(NH, 1))
    dvec = np.ascontiguousarray(
        np.asarray(inputs["L0_attn_D"], np.float32).reshape(1, NH))
    dtb = np.ascontiguousarray(
        np.asarray(inputs["L0_attn_dt_bias"], np.float32).reshape(NH, 1))
    gamma = np.ascontiguousarray(
        np.asarray(inputs["L0_attn_gamma"], np.float32).reshape(16, 128).T)
    tri01 = np.ascontiguousarray(
        np.triu(np.ones((128, 128), np.float32)))
    idf = np.ascontiguousarray(np.eye(128, dtype=np.float32))
    per = []
    for c in range(NCORES):
        lo = c * TREAL
        xs = np.zeros((FRAME, D), np.float32)
        seq_start = (c % 4 == 0)
        if seq_start:
            xs[128:] = x[lo:lo + TREAL]
            xs[:128] = x[lo:lo + 128]
        else:
            xs[:] = x[lo - 128:lo + TREAL]
        g = c % 4
        base = (c // 4) * 4
        inc_mask = np.zeros((1, 8), np.float32)
        for r in range(base, base + g):
            inc_mask[0, r] = 1.0
        sel_prev = np.zeros((1, 8), np.float32)
        if g > 0:
            sel_prev[0, base + g - 1] = 1.0
        per.append({
            "x": xs,
            "wg": np.ascontiguousarray(wg[c * 352:(c + 1) * 352]),
            "wu": np.ascontiguousarray(wu[c * 352:(c + 1) * 352]),
            "wd": np.ascontiguousarray(wd[c * 128:(c + 1) * 128]),
            "wipT": wipT, "woutT": woutT,
            "lnw": lnw, "lnb": lnb, "convw": convw, "convb": convb,
            "alog": alog, "dvec": dvec, "dtb": dtb, "gamma": gamma,
            "tri01": tri01, "idf": idf,
            "halo": np.array([[0.0 if seq_start else 1.0]], np.float32),
            "incmask": inc_mask, "selprev": sel_prev,
        })
    return per


def bt_read(full_ap, nblk):
    return full_ap.rearrange("p (b a) -> p b a", a=nblk)


def bcast16(small_ap, nb):
    return small_ap.unsqueeze(1).broadcast_to([small_ap.shape[0], 16, nb])


def build_program(debug_taps=()):
    nc = bass.Bass("TRN2", target_bir_lowering=False, debug=False)
    P = Prog()

    def din(name, shape, dt=F32):
        return nc.dram_tensor(name, shape, dt, kind="ExternalInput").ap()

    def dint(name, shape, dt=F32):
        return nc.dram_tensor(name, shape, dt, kind="Internal").ap()

    x_d = din("x", [FRAME, D])
    wg_d = din("wg", [352, D]); wu_d = din("wu", [352, D])
    wd_d = din("wd", [128, FF])
    wipT_d = din("wipT", [D, DIP]); woutT_d = din("woutT", [DI, D])
    lnw_d = din("lnw", [128, 8]); lnb_d = din("lnb", [128, 8])
    convw_d = din("convw", [128, 72]); convb_d = din("convb", [128, 18])
    alog_d = din("alog", [NH, 1]); dvec_d = din("dvec", [1, NH])
    dtb_d = din("dtb", [NH, 1]); gamma_d = din("gamma", [128, 16])
    tri_d = din("tri01", [128, 128]); idf_d = din("idf", [128, 128])
    halo_d = din("halo", [1, 1]); incm_d = din("incmask", [1, 8])
    selp_d = din("selprev", [1, 8])

    out_d = nc.dram_tensor("out", [TREAL, D], F32, kind="ExternalOutput").ap()

    x1s_d = dint("x1s", [FRAME, D])
    x2s_d = dint("x2s", [TREAL, D])
    SLE = 352 * D
    wpk_in = dint("wpk_in", [1, 3 * SLE], BF16)
    wpk_out = dint("wpk_out", [NCORES, 3 * SLE], BF16)
    wqg_in = wpk_in[0, 0 * SLE:1 * SLE].rearrange("(a b) -> a b", b=352)
    wqu_in = wpk_in[0, 1 * SLE:2 * SLE].rearrange("(a b) -> a b", b=352)
    wqd_in = wpk_in[0, 2 * SLE:3 * SLE].rearrange("(a b) -> a b", b=128)

    def wq_out_view(r, which):
        base = wpk_out[r, which * SLE:(which + 1) * SLE]
        return base.rearrange("(a b) -> a b", b=128 if which == 2 else 352)
    wqg_full = dint("wqg_full", [D, FF], BF16)
    wqu_full = dint("wqu_full", [D, FF], BF16)
    wqd_full = dint("wqd_full", [FF, D], BF16)
    ut_h = dint("ut_h", [8 * 128, FRAME], BF16)
    xsT_h = dint("xsT_h", [NCH * 128, DI], BF16)
    xd_h = dint("xd_h", [NCH * 128, DI], BF16)
    ds_h = dint("ds_h", [NCH * DS, DI])
    clb_h = dint("clb_h", [NCH, 128 * NH])
    lam_h = dint("lam_h", [1, NCH * NH])
    cll_h = dint("cll_h", [1, NCH * NH])
    coef_h = dint("coef_h", [1, NCORES * NH])
    bwz_h = dint("bwz_h", [1, DI])
    z_h = dint("z_h", [NCH * 128, DI], BF16)
    slocal_in = dint("slocal_in", [DS, DI + 32])
    slocal_out = dint("slocal_out", [NCORES * DS, DI + 32])

    ctx = ExitStack()

    def sb(name, shape, dt=F32):
        return ctx.enter_context(nc.sbuf_tensor(name, shape, dt))

    def sem(name):
        return Ctr(ctx.enter_context(nc.semaphore(name)))

    qs = sem("qs")
    qc = sem("qc")
    qg = sem("qg")
    vc = sem("vc"); sc = sem("sc"); tc = sem("tc"); gc = sem("gc")

    def dma(eng, out, in_, ctr=None):
        if ctr is None:
            ctr = {"sync": qs, "scalar": qc, "gpsimd": qg}[eng]
        return P.op(eng, lambda e, o=out, i=in_: e.dma_start(out=o, in_=i),
                    ctr, 16)

    def dmat(out, in_):
        return P.op("sync", lambda e, o=out, i=in_: e.dma_start_transpose(
            out=o, in_=i), qs, 16)

    def barrier():
        for eng in Prog.ENGS:
            for ctr in (qs, qc, qg, vc, sc, tc, gc):
                P.wait(eng, ctr, ctr.n)

    dbg_names = {}

    _const_tiles = {}

    def constap(val):
        if val in _const_tiles:
            return
        t = ctx.enter_context(nc.sbuf_tensor(
            f"cst{len(_const_tiles)}", [128, 1], F32))
        P.op("gpsimd", lambda e, t_=t, v_=val: e.memset(t_[:], v_), gc)
        nc.const_aps.register(F32, val, t[:])
        _const_tiles[val] = t

    def dbg(name, src_ap, shape, dt=F32):
        if name not in debug_taps or name in dbg_names:
            return
        key = "dbg_" + name
        d = nc.dram_tensor(key, shape, dt, kind="ExternalOutput").ap()
        dbg_names[name] = key
        for ctr in (vc, sc, tc, gc):
            P.wait("scalar", ctr, ctr.n)
        done = dma("scalar", d, src_ap)
        for eng in ("vector", "scalar", "tensor", "gpsimd", "sync"):
            P.wait(eng, qc, done)

    tri = sb("tri_s", [128, 128])
    idf = sb("idf_s", [128, 128])
    idb = sb("idb_s", [128, 128], BF16)
    onesb = sb("onesb", [1, 128], BF16)
    lnw_c = sb("lnw_c", [128, 8])
    lnbb = sb("lnbb", [128, 8], BF16)
    gam_c = sb("gam_c", [128, 16])
    convw_s = sb("convw_s", [128, 18 * 4])
    convb_s = sb("convb_s", [128, 18])
    alog_s = sb("alog_s", [NH, 1])
    acol_s = sb("acol_s", [NH, 1])
    dtb_s = sb("dtb_s", [NH, 1])
    bco_s = sb("bco_s", [NH, 1])
    lamacc = sb("lamacc", [NH, 1])
    dvec_rep = sb("dvec_rep", [128, NH])
    halo_c = sb("halo_c", [128, 1])
    incm_rep = sb("incm_rep", [NH, 8])
    selp_rep = sb("selp_rep", [NH, 8])
    bw_sb = sb("bw_sb", [128, 40])
    hbw = sb("hbw", [128, 18])
    cvb2 = sb("cvb2", [128, 18])
    dtb2 = sb("dtb2", [NH, 1])
    hc1 = sb("hc1", [128, 1])
    clT_all = sb("clT_all", [128, NCH * NH])
    dtT_all = sb("dtT_all", [128, NCH * NH])
    ECL_all = sb("ECL_all", [128, NCH * NH])
    F2_all = sb("F2_all", [128, NCH * NH])
    lamrep = sb("lamrep", [128, NCH * NH])
    bwz_sb = sb("bwz_sb", [1, DI], BF16)
    coefrep = sb("coefrep", [128, NCORES * NH])
    rmsc = sb("rmsc", [128, 4])

    stat_m1 = sb("stat_m1", [128, NT])
    stat_r1 = sb("stat_r1", [128, NT])
    stat_m2 = sb("stat_m2", [128, NT])
    stat_cu2 = sb("stat_cu2", [128, NT])
    stat_a1 = sb("stat_a1", [128, NT])
    stat_b1 = sb("stat_b1", [128, NT])
    stat_m3 = sb("stat_m3", [128, NT])
    stat_r3 = sb("stat_r3", [128, NT])
    stat_sq = sb("stat_sq", [128, NT])
    stat_t0 = sb("stat_t0", [128, NT])
    stat_t1 = sb("stat_t1", [128, NT])

    qbm = sb("qbm", [128, NB_F])
    qsc = sb("qsc", [128, NB_F])
    qrs = sb("qrs", [128, NB_F])
    wtr = sb("wtr", [128, 128], BF16)
    wtr2 = sb("wtr2", [128, 128], BF16)
    arena = sb("arena", [128, ARENA_W])

    class Ar:
        def __init__(self, start=0):
            self.off = start

        def f32(self, words):
            o = (self.off + 31) // 32 * 32
            self.off = o + words
            assert self.off <= ARENA_W, f"arena overflow {self.off}"
            return arena[:, o:o + words]

        def bf16(self, cols):
            w = (cols + 1) // 2
            o = (self.off + 31) // 32 * 32
            self.off = o + w
            assert self.off <= ARENA_W, f"arena overflow {self.off}"
            return arena[:, o:o + w].bitcast(BF16)

    AF_ = Ar()
    wgq = AF_.bf16(8 * FF)
    wuq = AF_.bf16(8 * FF)
    wdq = AF_.bf16(22 * D)
    xt0 = AF_.f32(D)
    xt0b = AF_.f32(D)
    ubt = AF_.f32(D)
    _r0 = AF_.off
    resb = AF_.f32(D)
    AF_.off = _r0
    qsm = AF_.f32(928)
    qs1 = AF_.f32(928)
    qs2 = AF_.f32(928)
    xqb = AF_.bf16(D)
    xqT = AF_.bf16(8 * 128)
    hsg = AF_.f32(512)
    hbt = AF_.f32(FF)
    hqb = AF_.bf16(FF)
    hqT = AF_.bf16(22 * 128)

    WQ = Ar(0)
    wld32 = WQ.f32(FF)
    wqsm = WQ.f32(FF // 2)
    wqs1 = WQ.f32(FF // 2)
    wqs2 = WQ.f32(FF // 2)

    AM = Ar()
    B_fm = AM.bf16(TREAL)
    C_fm = AM.bf16(TREAL)
    B_tm = AM.bf16(TREAL)
    GTm = AM.bf16(NCH * 128)
    XsT = AM.bf16(DI)
    Xd = AM.bf16(DI)
    dS_sb = AM.f32(DI)
    St = AM.f32(DI)
    m_res = AM.f32(D)
    overlay0 = AM.off

    S1 = Ar(overlay0)
    wipx = S1.bf16(8 * 2336)
    wipf = S1.f32(2336)
    u2b = S1.bf16(D)
    m_xt0 = S1.f32(D)
    s1b = S1.off

    A2R = Ar(s1b)
    wipr = A2R.bf16(8 * 2336)

    S1 = Ar(s1b)
    utsc = S1.bf16(8 * 520)
    xbc = [S1.f32(515) for _ in range(18)]
    cvt0 = S1.f32(512)
    cvt1 = S1.f32(512)
    xs_sc = [S1.bf16(512) for _ in range(16)]
    Xd2 = S1.bf16(DI)
    REP = S1.bf16(DI)
    dtraw = S1.f32(512)
    dt_s = S1.f32(512)
    sp_t1 = S1.f32(512)
    sp_t2 = S1.f32(512)
    cldt = S1.f32(512)
    clmask = S1.f32(512)
    cl_sc = S1.f32(512)
    cll_sc = S1.f32(16)
    cl4t = S1.f32(32)
    lam4 = S1.f32(32)

    ZP = Ar(overlay0)
    wipz = ZP.bf16(8 * DI)
    wipzr = ZP.bf16(8 * DI)
    zldz = ZP.f32(DI)
    utcz = ZP.bf16(8 * 128)
    zbuf = ZP.bf16(DI)
    bwt = ZP.f32(128)
    bwz32 = ZP.f32(DI)

    S2 = Ar(overlay0)
    wout = S2.bf16(16 * D)
    zldf = S2.f32(DI)
    zchunk = S2.bf16(DI)
    Sbf = S2.bf16(DI)
    BCTq = S2.f32(1024)
    Dq = S2.f32(1024)
    Dq2 = S2.f32(1024)
    MTa = S2.bf16(NH * 128)
    MTb = S2.bf16(NH * 128)
    Ew = S2.f32(1024)
    W1 = S2.f32(1024)
    W2 = S2.f32(1024)
    Yg = S2.f32(DI)
    Ygn = S2.bf16(DI)
    YgT = S2.bf16(16 * 128)
    drepb = S2.bf16(DI)
    Srt = S2.f32(DI)
    lam2 = S2.bf16(DI)
    x1t = S2.f32(D)
    x2t = S2.f32(D)
    Ltb = S2.f32(8)
    Ppb = S2.f32(8)
    on8 = S2.f32(8)
    Pme = S2.f32(1)
    cf8 = S2.f32(8)
    cft = S2.f32(32)

    psum = ctx.enter_context(nc.psum_tensor("psum", [128, 4096], F32))

    def bank(i, w=512):
        return psum[:, i * 512:i * 512 + w]

    def rep_expand(dst_view, src_small, inner, nsm, rows=128):
        ov = dst_view.rearrange("p (i r) -> p r i", i=nsm)
        iv = src_small.unsqueeze(1).broadcast_to([rows, inner, nsm])
        P.op("vector", lambda e: e.tensor_copy(ov, iv), vc)
        P.wait("vector", vc, vc.n)

    def _snap(width):
        P.op("vector", lambda e, w=width: e._custom_dve(
            QSNAP_A, out=qs1[:, :w], in0=qsm[:, :w],
            s0=0.0625, s1=0.5625, imm2=1.5625), vc)
        P.op("vector", lambda e, w=width: e._custom_dve(
            QSNAP_B, out=qs2[:, :w], in0=qsm[:, :w], in1=qs1[:, :w],
            s0=3.0625, s1=6.25, imm2=12.25), vc)
        P.op("vector", lambda e, w=width: e._custom_dve(
            QSNAP_C, out=qs1[:, :w], in0=qsm[:, :w], in1=qs2[:, :w],
            s0=25.0, s1=12.0), vc)

    def quant_bt_part(src_bt, nblk, a0, na, out_bf):
        w = 16 * na
        srcv = src_bt.rearrange("p (b a) -> p b a", a=nblk)[:, :, a0:a0 + na]
        P.op("vector", lambda e: e.tensor_reduce(
            qbm[:, a0:a0 + na],
            src_bt.rearrange("p (b a) -> p a b", a=nblk)[:, a0:a0 + na, :],
            AX, ALU.max, apply_absolute_value=True), vc)
        P.wait("vector", vc, vc.n)
        P.op("vector", lambda e: e.tensor_scalar(
            qsc[:, a0:a0 + na], qbm[:, a0:a0 + na], 1.0 / 12.0, 5e-13,
            ALU.mult, ALU.max), vc)
        P.wait("vector", vc, vc.n)
        P.op("vector", lambda e: e.reciprocal(
            qbm[:, a0:a0 + na], qsc[:, a0:a0 + na]), vc)
        P.wait("vector", vc, vc.n)
        P.op("vector", lambda e: e.tensor_scalar(
            qrs[:, a0:a0 + na], qbm[:, a0:a0 + na], 0.5, None, ALU.mult), vc)
        P.wait("vector", vc, vc.n)
        P.op("vector", lambda e, w_=w: e.tensor_tensor(
            qsm[:, :w_].rearrange("p (b a) -> p b a", a=na), srcv,
            bcast16(qrs[:, a0:a0 + na], na), ALU.mult), vc)
        P.wait("vector", vc, vc.n)
        _snap(w)
        outv = out_bf.rearrange("p (b a) -> p b a", a=nblk)[:, :, a0:a0 + na]
        P.op("vector", lambda e, w_=w: e.tensor_tensor(
            outv, qs1[:, :w_].rearrange("p (b a) -> p b a", a=na),
            bcast16(qsc[:, a0:a0 + na], na), ALU.mult), vc)

    def quant_nat_part(src_nat, nblk, a0, na, out_bf, rows=128):
        w = 16 * na
        pr = rows
        srcs = src_nat[:pr, 16 * a0:16 * a0 + w]
        P.op("vector", lambda e: e.tensor_reduce(
            qbm[:pr, a0:a0 + na],
            srcs.rearrange("p (a b) -> p a b", b=16),
            AX, ALU.max, apply_absolute_value=True), vc)
        P.wait("vector", vc, vc.n)
        P.op("vector", lambda e: e.tensor_scalar(
            qsc[:pr, a0:a0 + na], qbm[:pr, a0:a0 + na], 1.0 / 12.0, 5e-13,
            ALU.mult, ALU.max), vc)
        P.wait("vector", vc, vc.n)
        P.op("vector", lambda e: e.reciprocal(
            qbm[:pr, a0:a0 + na], qsc[:pr, a0:a0 + na]), vc)
        P.wait("vector", vc, vc.n)
        P.op("vector", lambda e: e.tensor_scalar(
            qrs[:pr, a0:a0 + na], qbm[:pr, a0:a0 + na], 0.5, None,
            ALU.mult), vc)
        P.wait("vector", vc, vc.n)
        P.op("vector", lambda e, w_=w: e.tensor_copy(
            wqs2[:pr, :w_].rearrange("p (a b) -> p b a", b=16),
            bcast16(qrs[:pr, a0:a0 + na], na)), vc)
        P.wait("vector", vc, vc.n)
        P.op("vector", lambda e, w_=w: e.tensor_tensor(
            wqsm[:pr, :w_], srcs, wqs2[:pr, :w_], ALU.mult), vc)
        P.op("vector", lambda e, w_=w, p_=pr: e._custom_dve(
            QSNAP_A, out=wqs1[:p_, :w_], in0=wqsm[:p_, :w_],
            s0=0.0625, s1=0.5625, imm2=1.5625), vc)
        P.op("vector", lambda e, w_=w, p_=pr: e._custom_dve(
            QSNAP_B, out=wqs2[:p_, :w_], in0=wqsm[:p_, :w_], in1=wqs1[:p_, :w_],
            s0=3.0625, s1=6.25, imm2=12.25), vc)
        P.op("vector", lambda e, w_=w, p_=pr: e._custom_dve(
            QSNAP_C, out=wqs1[:p_, :w_], in0=wqsm[:p_, :w_], in1=wqs2[:p_, :w_],
            s0=25.0, s1=12.0), vc)
        outv = out_bf[:pr, :].rearrange("p (b a) -> p b a", a=nblk)[
            :, :, a0:a0 + na]
        inv = wqs1[:pr, :w].rearrange("p (a b) -> p b a", b=16)
        P.op("vector", lambda e: e.tensor_tensor(
            outv, inv, bcast16(qsc[:pr, a0:a0 + na], na), ALU.mult), vc)

    dma("sync", tri[:], tri_d)
    dma("sync", idf[:], idf_d)
    dma("gpsimd", idb[:], idf_d)
    dma("gpsimd", lnbb[:], lnb_d)
    P.op("gpsimd", lambda e: e.memset(onesb[:], 1.0), gc)
    for _v in (EPS, -0.5, -1.0, 0.5, 1.0, 2.0):
        constap(_v)
    const_gc = gc.n
    P.wait("scalar", gc, const_gc)
    P.wait("vector", gc, const_gc)
    dma("sync", lnw_c[:], lnw_d)
    dma("sync", gam_c[:], gamma_d)
    dma("sync", convw_s[:], convw_d)
    dma("sync", convb_s[:], convb_d)
    dma("sync", alog_s[:], alog_d)
    dma("sync", dtb_s[:], dtb_d)
    dma("sync", dvec_rep[:], dvec_d.squeeze(0).partition_broadcast(128))
    dma("sync", halo_c[:], halo_d.squeeze(1).partition_broadcast(128))
    dma("sync", incm_rep[:], incm_d.squeeze(0).partition_broadcast(NH))
    dma("sync", selp_rep[:], selp_d.squeeze(0).partition_broadcast(NH))
    consts_loaded = qs.n

    def emit_wq(w_d, rows, width, nblk, dst_dram, col0=0, drow0=0):
        for it in range((rows + 127) // 128):
            r0 = it * 128
            pr = min(128, rows - r0)
            prev_s = qs.n
            P.wait("scalar", vc, vc.n)
            P.wait("scalar", qs, prev_s)
            ld = dma("scalar", wld32[:pr, :width],
                     w_d[r0:r0 + pr, col0:col0 + width])
            P.wait("vector", qc, ld)
            P.wait("vector", qs, prev_s)
            a0_ = 0
            while a0_ < nblk:
                na_ = min(88, nblk - a0_)
                quant_nat_part(wld32[:, :width], nblk, a0_, na_,
                               hqb[:, :width], rows=pr)
                a0_ += na_
            P.wait("vector", vc, vc.n)
            P.op("vector", lambda e: e.engine_nop(), vc)
            qdone = vc.n
            P.wait("sync", vc, qdone)
            for f in range(width // 128):
                tdone = dmat(wtr[:, :pr], hqb[:pr, f * 128:(f + 1) * 128])
                P.wait("sync", qs, tdone)
                dma("sync", dst_dram[drow0 + f * 128:drow0 + (f + 1) * 128,
                                     r0:r0 + pr], wtr[:, :pr])

    emit_wq(wg_d, 352, D, NB_D, wqg_in)
    emit_wq(wu_d, 352, D, NB_D, wqu_in)
    emit_wq(wd_d, 128, 1408, 88, wqd_in, col0=0, drow0=0)
    emit_wq(wd_d, 128, 1408, 88, wqd_in, col0=1408, drow0=1408)
    wq_stores = qs.n

    P.wait("gpsimd", qs, wq_stores)
    P.op("gpsimd", lambda e: e.collective_compute(
        "AllGather", ALU.bypass,
        ins=[wpk_in.squeeze(0)],
        outs=[wpk_out.rearrange("a b -> (a b)")],
        replica_groups=[list(range(NCORES))]), gc)
    P.wait("gpsimd", gc, gc.n)
    for r in range(NCORES):
        dma("gpsimd", wqg_full[:, r * 352:(r + 1) * 352], wq_out_view(r, 0))
        dma("gpsimd", wqu_full[:, r * 352:(r + 1) * 352], wq_out_view(r, 1))
        dma("gpsimd", wqd_full[:, r * 128:(r + 1) * 128], wq_out_view(r, 2))
    relayout_done = qg.n
    P.wait("sync", qg, relayout_done)

    def load_ffn_weights():
        for k in range(8):
            dma("sync", wgq[:, k * FF:(k + 1) * FF],
                wqg_full[k * 128:(k + 1) * 128, :])
            dma("sync", wuq[:, k * FF:(k + 1) * FF],
                wqu_full[k * 128:(k + 1) * 128, :])
        for k in range(22):
            dma("sync", wdq[:, k * D:(k + 1) * D],
                wqd_full[k * 128:(k + 1) * 128, :])
        return qs.n

    wq_loaded = load_ffn_weights()
    P.wait("scalar", qs, wq_loaded)
    dbg("wgq0", wgq[:, 0:FF], [128, FF], BF16)
    dbg("wdq0", wdq[:, 0:D], [128, D], BF16)

    def emit_stats(src_dram, m_dst, r_dst, ntiles, xa=None, xb=None):
        if xa is None:
            xa, xb = xt0, ubt
        for i in range(ntiles):
            P.wait("scalar", vc, vc.n)
            P.wait("scalar", sc, sc.n)
            ld = dma("scalar", xa[:], src_dram[i * 128:(i + 1) * 128, :])
            P.wait("vector", qc, ld)
            P.op("vector", lambda e, i_=i: e.reduce_sum(
                m_dst[:, i_:i_ + 1], xa[:], AX), vc)
            P.wait("scalar", qc, ld)
            P.op("scalar", lambda e, i_=i: e.activation(
                xb[:], xa[:], AF.Square,
                accum_out=stat_sq[:, i_:i_ + 1]), sc)
        return emit_stats_tail(m_dst, r_dst, ntiles)

    def emit_stats_tail(m_dst, r_dst, ntiles):
        P.wait("vector", sc, sc.n)
        P.wait("vector", vc, vc.n)
        P.op("vector", lambda e, n_=ntiles: e.tensor_scalar(
            m_dst[:, :n_], m_dst[:, :n_], 1.0 / D, None, ALU.mult), vc)
        P.op("vector", lambda e, n_=ntiles: e.tensor_scalar(
            stat_t1[:, :n_], stat_sq[:, :n_], 1.0 / D, None, ALU.mult), vc)
        v1 = vc.n
        P.wait("scalar", vc, v1)
        P.op("scalar", lambda e, n_=ntiles: e.activation(
            stat_t0[:, :n_], m_dst[:, :n_], AF.Square), sc)
        s1 = sc.n
        P.wait("vector", sc, s1)
        P.op("vector", lambda e, n_=ntiles: e.tensor_tensor(
            stat_t1[:, :n_], stat_t1[:, :n_], stat_t0[:, :n_],
            ALU.subtract), vc)
        v2 = vc.n
        P.wait("scalar", vc, v2)
        P.op("scalar", lambda e, n_=ntiles: e.activation(
            stat_sq[:, :n_], stat_t1[:, :n_], AF.Ln, bias=EPS), sc)
        P.op("scalar", lambda e, n_=ntiles: e.activation(
            r_dst[:, :n_], stat_sq[:, :n_], AF.Exp, scale=-0.5), sc)
        return sc.n

    ln1_sc = emit_stats(x_d, stat_m1, stat_r1, NT)
    dbg("statm", stat_m1[:], [128, NT])
    dbg("statsq", stat_sq[:], [128, NT])
    dbg("statv", stat_t1[:], [128, NT])
    dbg("statr", stat_r1[:], [128, NT])

    def emit_ffn(src_dram, res_dram, dst_dram, mcol, rcol, stats_sc, ntiles,
                 wq_ld, post_tile=None, res_scale=0.5, name=""):
        xts = [xt0, xt0b]
        marks = {"hb": 0, "dn": 0}

        def load_quant(i):
            xt = xts[i % 2]
            P.wait("scalar", vc, vc.n)
            P.wait("scalar", sc, sc.n)
            ldx = dma("scalar", xt[:], src_dram[i * 128:(i + 1) * 128, :])
            P.wait("vector", qc, ldx)
            if i == 0:
                P.wait("vector", sc, stats_sc)
            else:
                P.wait("vector", tc, marks["xqtr"])
            P.op("vector", lambda e, i_=i, x_=xt: e.tensor_scalar(
                ubt[:].rearrange("p (b a) -> p a b", a=NB_D),
                x_[:].rearrange("p (a b) -> p a b", b=16),
                mcol[:, i_:i_ + 1], rcol[:, i_:i_ + 1],
                ALU.subtract, ALU.mult), vc)
            P.wait("vector", vc, vc.n)
            quant_bt_part(ubt[:], NB_D, 0, 32, xqb[:])
            quant_bt_part(ubt[:], NB_D, 32, 32, xqb[:])
            P.wait("vector", vc, vc.n)
            P.op("vector", lambda e: e.engine_nop(), vc)
            marks["uq"] = vc.n
            if i == 1 and name == "1":
                dbg("u1", ubt[:], [128, D])
                dbg("xq1", xqb[:], [128, D], BF16)

        def mm_silu_hbuild(half):
            hb = 1408 * half
            for ic, (lo0, low) in enumerate(((0, 512), (512, 512),
                                             (1024, 384))):
                ci = 3 * half + ic
                pg, pu = ((0, 1), (3, 5))[ci % 2]
                hbk = "hb%d" % (ci % 2)
                P.wait("tensor", vc, marks.get(hbk, 0))
                P.wait("tensor", sc, marks.get("si%d" % (ci % 2), 0))
                for k in range(8):
                    P.op("tensor", lambda e, k_=k, o_=hb + lo0, w_=low,
                         b_=pg: e.matmul(
                             bank(b_, w_), xqT[:, k_ * 128:(k_ + 1) * 128],
                             wgq[:, k_ * FF + o_:k_ * FF + o_ + w_],
                             start=(k_ == 0), stop=(k_ == 7)), None)
                for k in range(8):
                    P.op("tensor", lambda e, k_=k, o_=hb + lo0, w_=low,
                         b_=pu: e.matmul(
                             bank(b_, w_), xqT[:, k_ * 128:(k_ + 1) * 128],
                             wuq[:, k_ * FF + o_:k_ * FF + o_ + w_],
                             start=(k_ == 0), stop=(k_ == 7)),
                         tc if k == 7 else None)
                mm_done = tc.n
                P.wait("scalar", tc, mm_done)
                P.wait("scalar", vc, marks["hb"])
                P.op("scalar", lambda e, w_=low, b_=pg: e.activation(
                    hsg[:, :w_], bank(b_, w_), AF.Silu), sc)
                marks["si%d" % (ci % 2)] = sc.n
                silu_done = sc.n
                P.wait("vector", sc, silu_done)
                P.wait("vector", tc, mm_done)
                a0 = lo0 // 16
                na = low // 16
                outv = hbt[:, hb:hb + 1408].rearrange(
                    "p (b a) -> p b a", a=88)[:, :, a0:a0 + na] \
                    .transpose([0, 2, 1])
                P.op("vector", lambda e, w_=low, o_=outv, b_=pu:
                     e.tensor_tensor(
                         o_,
                         hsg[:, :w_].rearrange("p (a b) -> p a b", b=16),
                         bank(b_, w_).rearrange("p (a b) -> p a b", b=16),
                         ALU.mult), vc)
                marks[hbk] = vc.n
                marks["hb"] = vc.n

        def quant_half(half, i):
            hb = 1408 * half
            P.wait("vector", vc, vc.n)
            if half == 0:
                P.wait("vector", qc, qc.n)
            quant_bt_part(hbt[:, hb:hb + 1408], 88, 0, 44,
                          hqb[:, hb:hb + 1408])
            quant_bt_part(hbt[:, hb:hb + 1408], 88, 44, 44,
                          hqb[:, hb:hb + 1408])
            P.wait("vector", vc, vc.n)
            P.op("vector", lambda e: e.engine_nop(), vc)
            marks["hq%d" % half] = vc.n
            if half == 1 and i == 1 and name == "1":
                dbg("h1", hbt[:], [128, FF])
                dbg("hq1", hqb[:], [128, FF], BF16)

        def tr_down(half):
            P.wait("tensor", vc, marks["hq%d" % half])
            evs = []
            for f in range(11):
                o_ = 3072 + 512 * (f % 2)
                if f >= 2:
                    P.wait("tensor", sc, evs[f - 2])
                P.op("tensor", lambda e, f_=11 * half + f, oo=o_:
                     e.transpose(
                         psum[:, oo:oo + 64].bitcast(BF16),
                         hqb[:, f_ * 128:(f_ + 1) * 128], idb[:]), tc)
                P.wait("scalar", tc, tc.n)
                P.op("scalar", lambda e, f_=11 * half + f, oo=o_:
                     e.activation(
                         hqT[:, f_ * 128:(f_ + 1) * 128],
                         psum[:, oo:oo + 64].bitcast(BF16), AF.Copy), sc)
                evs.append(sc.n)
            P.wait("tensor", sc, sc.n)
            if half == 0:
                P.wait("tensor", vc, marks["dn"])
            for k in range(11 * half, 11 * half + 11):
                for jo in range(2):
                    P.op("tensor", lambda e, k_=k, j_=jo: e.matmul(
                        bank(2 + 2 * j_, 512),
                        hqT[:, k_ * 128:(k_ + 1) * 128],
                        wdq[:, k_ * D + j_ * 512:k_ * D + (j_ + 1) * 512],
                        start=(k_ == 0), stop=(k_ == 21)),
                        tc if (k == 21 and jo == 1) else None)

        load_quant(0)
        for i in range(ntiles):
            P.wait("tensor", vc, marks["uq"])
            if i == 0:
                P.wait("tensor", qs, wq_ld)
            evs = []
            for f in range(8):
                o_ = 3072 + 512 * (f % 2)
                if f >= 2:
                    P.wait("tensor", sc, evs[f - 2])
                P.op("tensor", lambda e, f_=f, oo=o_: e.transpose(
                    psum[:, oo:oo + 64].bitcast(BF16),
                    xqb[:, f_ * 128:(f_ + 1) * 128], idb[:]), tc)
                P.wait("scalar", tc, tc.n)
                P.op("scalar", lambda e, f_=f, oo=o_: e.activation(
                    xqT[:, f_ * 128:(f_ + 1) * 128],
                    psum[:, oo:oo + 64].bitcast(BF16), AF.Copy), sc)
                evs.append(sc.n)
            marks["xqtr"] = tc.n
            P.wait("tensor", sc, sc.n)
            mm_silu_hbuild(0)
            quant_half(0, i)
            mm_silu_hbuild(1)
            tr_down(0)
            quant_half(1, i)
            tr_down(1)
            dn_done = tc.n
            if i + 1 < ntiles:
                load_quant(i + 1)
            P.wait("vector", tc, dn_done)
            P.wait("vector", qc, qc.n)
            P.op("vector", lambda e, rs_=res_scale, x_=xts[i % 2]:
                 e.scalar_tensor_tensor(
                     resb[:, :512], bank(2, 512), rs_, x_[:, :512],
                     ALU.mult, ALU.add), vc)
            P.op("vector", lambda e, rs_=res_scale, x_=xts[i % 2]:
                 e.scalar_tensor_tensor(
                     resb[:, 512:], bank(4, 512), rs_, x_[:, 512:],
                     ALU.mult, ALU.add), vc)
            res_done = vc.n
            marks["dn"] = vc.n
            if i == ntiles - 1 and name == "1":
                dbg("x1L", resb[:], [128, D])
            P.wait("scalar", vc, res_done)
            dma("scalar", dst_dram[i * 128:(i + 1) * 128, :], resb[:])
            if post_tile is not None:
                post_tile(i, res_done)

    emit_ffn(x_d, x_d, x1s_d, stat_m1, stat_r1, ln1_sc, NT, wq_loaded,
             name="1")

    barrier()
    emit_stats(x1s_d, stat_m2, stat_t0, NT, xa=m_xt0, xb=m_res)
    P.wait("vector", sc, sc.n)
    P.op("vector", lambda e: e.tensor_scalar(
        stat_cu2[:], stat_t1[:], EPS, None, ALU.add), vc)
    P.wait("vector", vc, vc.n)
    P.op("vector", lambda e: e.reciprocal(stat_b1[:], stat_cu2[:]), vc)
    P.wait("vector", vc, vc.n)
    P.op("vector", lambda e: e.tensor_tensor(
        stat_cu2[:], stat_t1[:], stat_b1[:], ALU.mult), vc)
    _v = vc.n
    P.wait("scalar", vc, _v)
    P.op("scalar", lambda e: e.activation(
        stat_b1[:], stat_cu2[:], AF.Ln, bias=EPS), sc)
    P.op("scalar", lambda e: e.activation(
        stat_cu2[:], stat_b1[:], AF.Exp, scale=-0.5), sc)
    _s = sc.n
    P.wait("vector", sc, _s)
    P.op("vector", lambda e: e.tensor_tensor(
        stat_cu2[:], stat_cu2[:], stat_t0[:], ALU.mult), vc)
    P.op("vector", lambda e: e.tensor_scalar(
        stat_a1[:], stat_t0[:], 1.0, None, ALU.add), vc)
    P.op("vector", lambda e: e.tensor_tensor(
        stat_b1[:], stat_m2[:], stat_t0[:], ALU.mult), vc)
    P.wait("vector", vc, vc.n)
    P.op("vector", lambda e: e.tensor_scalar(
        stat_b1[:], stat_b1[:], -1.0, None, ALU.mult), vc)
    P.op("scalar", lambda e: e.activation(bco_s[:], alog_s[:], AF.Exp), sc)
    P.wait("vector", sc, sc.n)
    P.op("vector", lambda e: e.tensor_scalar(
        acol_s[:], bco_s[:], -1.0, None, ALU.mult), vc)
    P.op("vector", lambda e: e.memset(lamacc[:], 0.0), vc)

    for k in range(8):
        P.wait("scalar", vc, vc.n)
        ld = dma("scalar", wipf[:],
                 wipT_d[k * 128:(k + 1) * 128, 2048:DIP])
        P.wait("vector", qc, ld)
        P.op("vector", lambda e, k_=k: e.tensor_scalar(
            wipx[:, k_ * 2336:(k_ + 1) * 2336],
            wipf[:], lnw_c[:, k_:k_ + 1], None, ALU.mult), vc)
        P.op("vector", lambda e, k_=k: e.tensor_copy(
            wipr[:, k_ * 2336:(k_ + 1) * 2336], wipf[:]), vc)
    wip_folded = vc.n

    P.wait("tensor", vc, wip_folded)
    P.wait("tensor", qg, qg.n)
    for f in range(19):
        ow = 128 if f < 18 else 32
        for k in range(8):
            P.op("tensor", lambda e, f_=f, k_=k, w_=ow: e.matmul(
                psum[:w_, 3072:3073],
                wipr[:, k_ * 2336 + f_ * 128:k_ * 2336 + f_ * 128 + w_],
                lnbb[:, k_:k_ + 1],
                start=(k_ == 0), stop=(k_ == 7)),
                tc if k == 7 else None)
        bwdone = tc.n
        P.wait("vector", tc, bwdone)
        P.op("vector", lambda e, f_=f, w_=ow: e.tensor_copy(
            bw_sb[:w_, 16 + f_:17 + f_], psum[:w_, 3072:3073]), vc)
        P.wait("tensor", vc, vc.n)

    P.wait("vector", sc, sc.n)
    P.op("vector", lambda e: e.tensor_scalar(
        hc1[:, 0:1], halo_c[:, 0:1], -1.0, None, ALU.add), vc)
    P.op("vector", lambda e: e.tensor_tensor(
        dtb2[:32, 0:1], dtb_s[:32, 0:1], bw_sb[:32, 34:35], ALU.add), vc)
    P.op("vector", lambda e: e.tensor_tensor(
        cvb2[:, 0:18],
        convw_s[:].rearrange("p (f k) -> p f k", k=4)[:, :, 0],
        convw_s[:].rearrange("p (f k) -> p f k", k=4)[:, :, 1],
        ALU.add), vc)
    P.op("vector", lambda e: e.tensor_tensor(
        hbw[:, 0:18],
        convw_s[:].rearrange("p (f k) -> p f k", k=4)[:, :, 2],
        convw_s[:].rearrange("p (f k) -> p f k", k=4)[:, :, 3],
        ALU.add), vc)
    P.wait("vector", vc, vc.n)
    P.op("vector", lambda e: e.tensor_tensor(
        cvb2[:, 0:18], cvb2[:, 0:18], hbw[:, 0:18], ALU.add), vc)
    P.wait("vector", vc, vc.n)
    P.op("vector", lambda e: e.tensor_tensor(
        cvb2[:, 0:18], cvb2[:, 0:18], bw_sb[:, 16:34], ALU.mult), vc)
    P.wait("vector", vc, vc.n)
    P.op("vector", lambda e: e.tensor_tensor(
        cvb2[:, 0:18], cvb2[:, 0:18], convb_s[:, 0:18], ALU.add), vc)
    P.op("vector", lambda e: e.tensor_scalar(
        hbw[:, 0:18], bw_sb[:, 16:34], hc1[:, 0:1], None, ALU.mult), vc)

    for i in range(NT):
        P.wait("scalar", vc, vc.n)
        P.wait("scalar", qs, qs.n)
        ldx = dma("scalar", m_xt0[:], x1s_d[i * 128:(i + 1) * 128, :])
        P.wait("vector", qc, ldx)
        P.wait("vector", qs, qs.n)
        P.op("vector", lambda e, i_=i: e.tensor_scalar(
            u2b[:], m_xt0[:], stat_m2[:, i_:i_ + 1], stat_cu2[:, i_:i_ + 1],
            ALU.subtract, ALU.mult), vc)
        ud = vc.n
        P.wait("tensor", vc, ud)
        evs = []
        sts = []
        for f in range(8):
            o_ = 3072 + 512 * (f % 2)
            wt = wtr if f % 2 == 0 else wtr2
            if f >= 2:
                P.wait("tensor", sc, evs[f - 2])
            P.op("tensor", lambda e, f_=f, oo=o_: e.transpose(
                psum[:, oo:oo + 64].bitcast(BF16),
                u2b[:, f_ * 128:(f_ + 1) * 128], idb[:]), tc)
            P.wait("scalar", tc, tc.n)
            if f >= 2:
                P.wait("scalar", qs, sts[f - 2])
            P.op("scalar", lambda e, oo=o_, w_=wt: e.activation(
                w_[:], psum[:, oo:oo + 64].bitcast(BF16), AF.Copy), sc)
            evs.append(sc.n)
            P.wait("sync", sc, sc.n)
            dma("sync", ut_h[f * 128:(f + 1) * 128, i * 128:(i + 1) * 128],
                wt[:])
            sts.append(qs.n)
    ut_done = qs.n
    dbg("u2b", u2b[:], [128, D], BF16)

    barrier()
    P.op("vector", lambda e: e.memset(St[:], 0.0), vc)
    P.op("vector", lambda e: e.memset(clmask[:32, :], 1.0), vc)
    for kk in range(4):
        P.op("vector", lambda e, k_=kk: e.memset(
            clmask[:32, k_ * 128:k_ * 128 + 1], 0.0), vc)
    sw1_init = vc.n

    for sci in range(NSC):
        fr0 = 125 + 512 * sci
        P.wait("sync", vc, vc.n)
        P.wait("sync", tc, tc.n)
        for k in range(8):
            dma("sync", utsc[:, k * 520:k * 520 + 515],
                ut_h[k * 128:(k + 1) * 128, fr0:fr0 + 515])
        ut_ld = qs.n
        if sci == 0:
            P.wait("tensor", vc, sw1_init)
        P.wait("tensor", qs, ut_ld)
        if sci > 0:
            P.wait("vector", sc, sc.n)
            for f in range(18):
                P.op("vector", lambda e, f_=f: e.tensor_copy(
                    xbc[f_][:, 0:3], xbc[f_][:, 512:515]), vc)
            halo_cp = vc.n
        for f in range(19):
            ow = 128 if f < 18 else 32
            bnk = f % 2
            P.wait("tensor", sc, sc.n)
            if sci > 0 and f < 2:
                P.wait("tensor", vc, halo_cp)
            for k in range(8):
                P.op("tensor", lambda e, f_=f, k_=k, w_=ow, b_=bnk: e.matmul(
                    bank(b_, 512)[:w_, :],
                    wipx[:, k_ * 2336 + f_ * 128:k_ * 2336 + f_ * 128 + w_],
                    utsc[:, k_ * 520 + 3:k_ * 520 + 515],
                    start=(k_ == 0), stop=(k_ == 7)),
                    None if (sci == 0 and f < 18) else (tc if k == 7 else None))
            if sci == 0 and f < 18:
                for k in range(8):
                    P.op("tensor", lambda e, f_=f, k_=k, w_=ow: e.matmul(
                        psum[:w_, 1024:1027],
                        wipx[:, k_ * 2336 + f_ * 128:
                             k_ * 2336 + f_ * 128 + w_],
                        utsc[:, k_ * 520:k_ * 520 + 3],
                        start=(k_ == 0), stop=(k_ == 7)),
                        tc if k == 7 else None)
            mmd = tc.n
            P.wait("scalar", tc, mmd)
            if f < 18:
                P.op("scalar", lambda e, f_=f, b_=bnk: e.activation(
                    xbc[f_][:, 3:515], bank(b_, 512), AF.Copy), sc)
                if sci == 0:
                    P.op("scalar", lambda e, f_=f: e.activation(
                        xbc[f_][:, 0:3], psum[:, 1024:1027], AF.Copy), sc)
                    hm = sc.n
                    P.wait("vector", sc, hm)
                    P.op("vector", lambda e, f_=f: e.tensor_scalar(
                        xbc[f_][:, 0:3], xbc[f_][:, 0:3], halo_c[:, 0:1],
                        hbw[:, f_:f_ + 1], ALU.mult, ALU.add), vc)
            else:
                P.op("scalar", lambda e, b_=bnk: e.activation(
                    dtraw[:32, :], bank(b_, 512)[:32, :], AF.Copy), sc)
        evac_done = sc.n
        if sci == 0:
            dbg("xbc0", xbc[0][:, 0:515], [128, 515])
            dbg("dtraw0", dtraw[:32, :], [32, 512])
        P.wait("vector", sc, evac_done)
        P.op("vector", lambda e: e.tensor_scalar(
            sp_t1[:32, :], dtraw[:32, :], dtb2[:32, 0:1], None, ALU.add), vc)
        P.op("vector", lambda e: e.tensor_scalar(
            sp_t2[:32, :], sp_t1[:32, :], -1.0, None, ALU.mult), vc)
        P.op("vector", lambda e: e.tensor_tensor(
            cvt0[:32, :], sp_t1[:32, :], sp_t2[:32, :], ALU.min), vc)
        nadone = vc.n
        P.wait("scalar", vc, nadone)
        P.op("scalar", lambda e: e.activation(
            cvt1[:32, :], cvt0[:32, :], AF.Exp), sc)
        P.op("scalar", lambda e: e.activation(
            sp_t2[:32, :], cvt1[:32, :], AF.Ln, bias=1.0), sc)
        lndone = sc.n
        P.op("vector", lambda e: e.tensor_scalar(
            dtraw[:32, :], sp_t1[:32, :], 0.0, None, ALU.max), vc)
        P.wait("vector", sc, lndone)
        P.op("vector", lambda e: e.tensor_tensor(
            dt_s[:32, :], dtraw[:32, :], sp_t2[:32, :], ALU.add), vc)
        P.wait("vector", vc, vc.n)
        P.op("vector", lambda e: e.tensor_scalar(
            sp_t1[:32, :], dt_s[:32, :], acol_s[:32, 0:1], None,
            ALU.mult), vc)
        P.op("vector", lambda e: e.tensor_tensor_scan(
            cl_sc[:32, :], clmask[:32, :], sp_t1[:32, :], 0.0,
            ALU.mult, ALU.add), vc)
        cl_done = vc.n
        if sci == 0:
            dbg("dts0", dt_s[:32, :], [32, 512])
            dbg("cls0", cl_sc[:32, :], [32, 512])
        P.wait("vector", vc, cl_done)
        P.op("vector", lambda e: e.tensor_copy(cldt[0:32, :], cl_sc[:32, :]),
             vc)
        P.op("vector", lambda e: e.tensor_copy(cldt[32:64, :], dt_s[:32, :]),
             vc)
        cldt_done = vc.n
        P.wait("tensor", vc, cldt_done)
        for jj in range(4):
            j = 4 * sci + jj
            P.wait("tensor", vc, vc.n)
            P.op("tensor", lambda e, jj_=jj: e.transpose(
                psum[:, 1280:1344], cldt[0:64, 128 * jj_:128 * (jj_ + 1)],
                idf[0:64, 0:64]), tc)
            trd = tc.n
            P.wait("vector", tc, trd)
            P.op("vector", lambda e, j_=j: e.tensor_copy(
                clT_all[:, 32 * j_:32 * (j_ + 1)], psum[:, 1280:1312]), vc)
            P.op("vector", lambda e, j_=j: e.tensor_copy(
                dtT_all[:, 32 * j_:32 * (j_ + 1)], psum[:, 1312:1344]), vc)
        trans_done = vc.n
        P.wait("vector", vc, trans_done)
        P.op("vector", lambda e, s_=sci: e.tensor_copy(
            cll_sc[:32, 0:4],
            cl_sc[:32, :].rearrange("p (k q) -> p k q", q=128)
            [:, :, 127:128].squeeze(2)), vc)
        P.wait("vector", vc, vc.n)
        P.op("vector", lambda e: e.reduce_sum(
            sp_t2[:32, 0:1], cll_sc[:32, 0:4], AX), vc)
        P.wait("vector", vc, vc.n)
        P.op("vector", lambda e: e.tensor_tensor(
            lamacc[:32, 0:1], lamacc[:32, 0:1], sp_t2[:32, 0:1],
            ALU.add), vc)
        P.wait("tensor", vc, vc.n)
        P.op("tensor", lambda e: e.transpose(
            psum[:4, 1280:1312], cll_sc[:32, 0:4], idf[0:32, 0:32]), tc)
        cl4_tr = tc.n
        P.wait("vector", tc, cl4_tr)
        P.op("vector", lambda e: e.tensor_copy(cl4t[0:4, :],
                                               psum[:4, 1280:1312]), vc)
        P.wait("scalar", vc, vc.n)
        P.op("scalar", lambda e: e.activation(
            lam4[0:4, :], cl4t[0:4, :], AF.Exp), sc)
        lam_done = sc.n
        P.wait("scalar", sc, lam_done)
        st1 = dma("scalar",
                  lam_h[0, 128 * sci:128 * (sci + 1)]
                  .rearrange("(k h) -> k h", h=NH), lam4[0:4, :])
        st2 = dma("scalar",
                  cll_h[0, 128 * sci:128 * (sci + 1)]
                  .rearrange("(k h) -> k h", h=NH), cl4t[0:4, :])
        P.wait("gpsimd", qc, st2)
        b1_ = dma("gpsimd", lamrep[:, 128 * sci:128 * (sci + 1)],
                  lam_h[0, 128 * sci:128 * (sci + 1)]
                  .partition_broadcast(128))
        b2_ = dma("gpsimd", coefrep[:, 0:128],
                  cll_h[0, 128 * sci:128 * (sci + 1)]
                  .partition_broadcast(128))
        P.wait("scalar", vc, trans_done)
        P.op("scalar", lambda e, s_=sci: e.activation(
            ECL_all[:, 128 * s_:128 * (s_ + 1)],
            clT_all[:, 128 * s_:128 * (s_ + 1)], AF.Exp), sc)
        P.wait("vector", qg, b2_)
        P.op("vector", lambda e, s_=sci: e.tensor_tensor(
            cvt0[:, :128], coefrep[:, 0:128],
            clT_all[:, 128 * s_:128 * (s_ + 1)], ALU.subtract), vc)
        f2v = vc.n
        P.wait("scalar", vc, f2v)
        P.op("scalar", lambda e: e.activation(
            cvt1[:, :128], cvt0[:, :128], AF.Exp), sc)
        f2s = sc.n
        P.wait("vector", sc, f2s)
        P.op("vector", lambda e, s_=sci: e.tensor_tensor(
            F2_all[:, 128 * s_:128 * (s_ + 1)],
            dtT_all[:, 128 * s_:128 * (s_ + 1)], cvt1[:, :128],
            ALU.mult), vc)
        if sci == 0:
            dbg("f2a", F2_all[:, 0:128], [128, 128])
            dbg("lamr", lamrep[:, 0:128], [128, 128])
        P.wait("vector", vc, trans_done)
        P.op("vector", lambda e, s_=sci: e.tensor_scalar(
            cvt0[:, :128], clT_all[:, 128 * s_:128 * (s_ + 1)], -1.0,
            None, ALU.mult), vc)
        ncl_done = vc.n
        for jj in range(4):
            j = 4 * sci + jj
            P.wait("sync", vc, ncl_done)
            dma("sync", clb_h[j:j + 1, :].squeeze(0)
                .rearrange("(t h) -> t h", h=NH),
                cvt0[:, 32 * jj:32 * (jj + 1)])
        P.wait("vector", sc, sc.n)
        P.wait("vector", qs, qs.n)
        for f in range(18):
            P.op("vector", lambda e, f_=f: e.tensor_scalar(
                cvt0[:, :], xbc[f_][:, 0:512], convw_s[:, 4 * f_:4 * f_ + 1],
                None, ALU.mult), vc)
            P.op("vector", lambda e, f_=f: e.scalar_tensor_tensor(
                cvt1[:, :], xbc[f_][:, 1:513], convw_s[:, 4 * f_ + 1:4 * f_ + 2],
                cvt0[:, :], ALU.mult, ALU.add), vc)
            P.op("vector", lambda e, f_=f: e.scalar_tensor_tensor(
                cvt0[:, :], xbc[f_][:, 2:514], convw_s[:, 4 * f_ + 2:4 * f_ + 3],
                cvt1[:, :], ALU.mult, ALU.add), vc)
            P.op("vector", lambda e, f_=f: e.scalar_tensor_tensor(
                cvt1[:, :], xbc[f_][:, 3:515], convw_s[:, 4 * f_ + 3:4 * f_ + 4],
                cvt0[:, :], ALU.mult, ALU.add), vc)
            cvd = vc.n
            P.wait("scalar", vc, cvd)
            if f < 16:
                P.op("scalar", lambda e, f_=f: e.activation(
                    xs_sc[f_][:, :], cvt1[:, :], AF.Silu,
                    bias=cvb2[:, f_:f_ + 1]), sc)
            elif f == 16:
                P.op("scalar", lambda e, s_=sci: e.activation(
                    B_fm[:, 512 * s_:512 * (s_ + 1)], cvt1[:, :], AF.Silu,
                    bias=cvb2[:, 16:17]), sc)
            else:
                P.op("scalar", lambda e, s_=sci: e.activation(
                    C_fm[:, 512 * s_:512 * (s_ + 1)], cvt1[:, :], AF.Silu,
                    bias=cvb2[:, 17:18]), sc)
            P.wait("vector", sc, sc.n)
        conv_done = sc.n
        if sci == 0:
            dbg("xs00", xs_sc[0][:, :], [128, 512], BF16)
            dbg("bfm0", B_fm[:, 0:512], [128, 512], BF16)
        for jj in range(4):
            j = 4 * sci + jj
            P.wait("tensor", sc, conv_done)
            P.wait("tensor", vc, vc.n)
            P.wait("tensor", qs, qs.n)
            evs = []
            for f in range(17):
                o_ = 3072 + 512 * (f % 2)
                if f >= 2:
                    P.wait("tensor", sc, evs[f - 2])
                if f < 16:
                    src_ = xs_sc[f][:, 128 * jj:128 * (jj + 1)]
                else:
                    src_ = B_fm[:, 128 * j:128 * (j + 1)]
                P.op("tensor", lambda e, s_=src_, oo=o_: e.transpose(
                    psum[:, oo:oo + 64].bitcast(BF16), s_, idb[:]), tc)
                P.wait("scalar", tc, tc.n)
                if f < 16:
                    dst_ = XsT[:, f * 128:(f + 1) * 128]
                else:
                    dst_ = B_tm[:, 128 * j:128 * (j + 1)]
                P.op("scalar", lambda e, d_=dst_, oo=o_: e.activation(
                    d_, psum[:, oo:oo + 64].bitcast(BF16), AF.Copy), sc)
                evs.append(sc.n)
            xsT_done = sc.n
            P.wait("sync", sc, xsT_done)
            dma("sync", xsT_h[j * 128:(j + 1) * 128, :], XsT[:])
            P.wait("vector", sc, xsT_done)
            xdv = Xd[:].rearrange("p (i r) -> p r i", i=NH)
            xsv = XsT[:].rearrange("p (i r) -> p r i", i=NH)
            dtv = dtT_all[:, 32 * j:32 * (j + 1)].unsqueeze(1) \
                .broadcast_to([128, HD, NH])
            P.op("vector", lambda e, o_=xdv, a_=xsv, b_=dtv: e.tensor_tensor(
                o_, a_, b_, ALU.mult), vc)
            P.wait("vector", vc, vc.n)
            xd_done = vc.n
            P.wait("sync", vc, xd_done)
            dma("sync", xd_h[j * 128:(j + 1) * 128, :], Xd[:])
            x2v = Xd2[:].rearrange("p (i r) -> p r i", i=NH)
            f2v = F2_all[:, 32 * j:32 * (j + 1)].unsqueeze(1) \
                .broadcast_to([128, HD, NH])
            P.op("vector", lambda e, o_=x2v, a_=xsv, b_=f2v: e.tensor_tensor(
                o_, a_, b_, ALU.mult), vc)
            P.wait("vector", vc, vc.n)
            xd2_done = vc.n
            if j == 0:
                dbg("xst0", XsT[:], [128, DI], BF16)
                dbg("xd20", Xd2[:], [128, DI], BF16)
            P.wait("tensor", qs, xsT_done)
            P.wait("tensor", vc, vc.n)
            P.op("tensor", lambda e, j_=j: e.matmul(
                psum[:, 1152:1280], B_fm[:, 128 * j_:128 * (j_ + 1)],
                C_fm[:, 128 * j_:128 * (j_ + 1)], start=True, stop=True), tc)
            gmm = tc.n
            P.wait("vector", tc, gmm)
            P.op("vector", lambda e, j_=j: e.tensor_tensor(
                GTm[:, 128 * j_:128 * (j_ + 1)], psum[:, 1152:1280],
                tri[:], ALU.mult), vc)
            P.wait("tensor", vc, vc.n)
            P.wait("tensor", sc, sc.n)
            for g4 in range(4):
                P.op("tensor", lambda e, j_=j, g_=g4: e.matmul(
                    bank(4 + g_, 512), B_tm[:, 128 * j_:128 * (j_ + 1)],
                    Xd2[:, 512 * g_:512 * (g_ + 1)],
                    start=True, stop=True), tc if g4 == 3 else None)
            dsd = tc.n
            P.wait("scalar", tc, dsd)
            for g4 in range(4):
                P.op("scalar", lambda e, g_=g4: e.activation(
                    dS_sb[:, 512 * g_:512 * (g_ + 1)], bank(4 + g_, 512),
                    AF.Copy), sc)
            dse = sc.n
            P.wait("scalar", sc, dse)
            dma("scalar", ds_h[j * 128:(j + 1) * 128, :], dS_sb[:])
            if j == 0:
                dbg("gtm0", GTm[:, 0:128], [128, 128], BF16)
                dbg("ds0", dS_sb[:], [128, DI])
            P.wait("vector", sc, dse)
            P.wait("vector", qg, b1_)
            stv = St[:].rearrange("p (i r) -> p r i", i=NH)
            lmv = lamrep[:, 32 * j:32 * (j + 1)].unsqueeze(1) \
                .broadcast_to([128, HD, NH])
            P.op("vector", lambda e, o_=stv, b_=lmv: e.tensor_tensor(
                o_, o_, b_, ALU.mult), vc)
            P.wait("vector", vc, vc.n)
            P.op("vector", lambda e: e.tensor_tensor(
                St[:], St[:], dS_sb[:], ALU.add), vc)
            P.wait("vector", vc, vc.n)
    slw = vc.n
    P.wait("scalar", vc, slw)
    dma("scalar", slocal_in[:, 0:DI], St[:])
    dma("scalar", slocal_in[0:1, DI:DI + 32].squeeze(0),
        lamacc[:32, 0:1].squeeze(1))
    sl_st = qc.n
    dbg("slocal", St[:], [128, DI])
    P.wait("gpsimd", qc, sl_st)
    P.op("gpsimd", lambda e: e.collective_compute(
        "AllGather", ALU.bypass,
        ins=[slocal_in.rearrange("a b -> (a b)")],
        outs=[slocal_out.rearrange("a b -> (a b)")],
        replica_groups=[list(range(NCORES))]), gc)
    sl_gathered = gc.n

    for eng in Prog.ENGS:
        if eng == "gpsimd":
            continue
        for ctr in (qs, qc, vc, sc, tc):
            P.wait(eng, ctr, ctr.n)
    for k in range(8):
        P.wait("scalar", vc, vc.n)
        ld = dma("scalar", zldz[:], wipT_d[k * 128:(k + 1) * 128, 0:DI])
        P.wait("vector", qc, ld)
        P.op("vector", lambda e, k_=k: e.tensor_scalar(
            wipz[:, k_ * DI:(k_ + 1) * DI], zldz[:],
            lnw_c[:, k_:k_ + 1], None, ALU.mult), vc)
        P.op("vector", lambda e, k_=k: e.tensor_copy(
            wipzr[:, k_ * DI:(k_ + 1) * DI], zldz[:]), vc)
    zfold_done = vc.n
    P.wait("tensor", vc, zfold_done)
    for f in range(16):
        for k in range(8):
            P.op("tensor", lambda e, f_=f, k_=k: e.matmul(
                psum[:128, 3072:3073],
                wipzr[:, k_ * DI + f_ * 128:k_ * DI + f_ * 128 + 128],
                lnbb[:, k_:k_ + 1],
                start=(k_ == 0), stop=(k_ == 7)),
                tc if k == 7 else None)
        bwd = tc.n
        P.wait("vector", tc, bwd)
        P.op("vector", lambda e, f_=f: e.tensor_copy(
            bw_sb[:, f_:f_ + 1], psum[:, 3072:3073]), vc)
        P.wait("tensor", vc, vc.n)
    bwz_done = vc.n
    P.wait("tensor", vc, bwz_done)
    P.op("tensor", lambda e: e.transpose(
        psum[:16, 1280:1408], bw_sb[:, 0:16], idf[:]), tc)
    bwt_tr = tc.n
    P.wait("vector", tc, bwt_tr)
    P.op("vector", lambda e: e.tensor_copy(bwt[0:16, :],
                                           psum[:16, 1280:1408]), vc)
    P.wait("scalar", vc, vc.n)
    dma("scalar", bwz_h[0, :].rearrange("(t p) -> t p", p=128),
        bwt[0:16, :])
    bwz_st = qc.n
    P.wait("sync", qc, bwz_st)
    bwld = dma("sync", bwz32[0:1, :], bwz_h)
    P.wait("vector", qs, bwld)
    P.op("vector", lambda e: e.tensor_copy(bwz_sb[:], bwz32[0:1, :]), vc)
    P.wait("tensor", vc, vc.n)
    for j in range(NCH):
        P.wait("sync", tc, tc.n)
        for k in range(8):
            dma("sync", utcz[:, k * 128:(k + 1) * 128],
                ut_h[k * 128:(k + 1) * 128, 128 + 128 * j:256 + 128 * j])
        utl = qs.n
        P.wait("tensor", qs, utl)
        P.wait("tensor", sc, sc.n)
        for g4 in range(4):
            for k in range(8):
                P.op("tensor", lambda e, k_=k, g_=g4: e.matmul(
                    bank(g_, 512), utcz[:, k_ * 128:(k_ + 1) * 128],
                    wipz[:, k_ * DI + 512 * g_:k_ * DI + 512 * (g_ + 1)],
                    start=(k_ == 0), stop=False), None)
            P.op("tensor", lambda e, g_=g4: e.matmul(
                bank(g_, 512), onesb[:, :],
                bwz_sb[:, 512 * g_:512 * (g_ + 1)],
                start=False, stop=True), tc if g4 == 3 else None)
        zmm = tc.n
        P.wait("scalar", tc, zmm)
        P.wait("scalar", qs, qs.n)
        for g4 in range(4):
            P.op("scalar", lambda e, g_=g4: e.activation(
                zbuf[:, 512 * g_:512 * (g_ + 1)], bank(g_, 512),
                AF.Silu), sc)
        zev = sc.n
        P.wait("sync", sc, zev)
        dma("sync", z_h[j * 128:(j + 1) * 128, :], zbuf[:])

    barrier()
    for k in range(16):
        P.wait("scalar", vc, vc.n)
        ld = dma("scalar", zldf[:, 0:D], woutT_d[k * 128:(k + 1) * 128, :])
        P.wait("vector", qc, ld)
        P.op("vector", lambda e, k_=k: e.tensor_scalar(
            wout[:, k_ * D:(k_ + 1) * D], zldf[:, 0:D],
            gam_c[:, k_:k_ + 1], None, ALU.mult), vc)
        if k == 15:
            P.wait("scalar", qc, ld)
            dbg("zld15", zldf[:, 0:D], [128, D])
    folds_done = vc.n
    P.wait("scalar", vc, folds_done)
    dbg("wout0", wout[:, 0:D], [128, D], BF16)
    dbg("gamc", gam_c[:], [128, 16])
    P.op("vector", lambda e: e.tensor_copy(
        drepb[:].rearrange("q (h r) -> q r h", h=NH),
        dvec_rep[:].unsqueeze(1).broadcast_to([128, HD, NH])), vc)

    P.wait("sync", gc, sl_gathered)
    for r in range(NCORES):
        dma("sync", Ltb[:32, r:r + 1].squeeze(1),
            slocal_out[r * DS:r * DS + 1, DI:DI + 32].squeeze(0))
    lt_ld = qs.n
    P.wait("vector", qs, lt_ld)
    dbg("ltb", Ltb[:32, :], [32, 8])
    P.op("vector", lambda e: e.memset(on8[:32, :], 1.0), vc)
    P.wait("vector", vc, vc.n)
    P.op("vector", lambda e: e.tensor_tensor_scan(
        Ppb[:32, :], on8[:32, :], Ltb[:32, :], 0.0, ALU.mult, ALU.add), vc)
    P.wait("vector", vc, vc.n)
    P.op("vector", lambda e: e.tensor_tensor(
        cf8[:32, :], Ppb[:32, :], selp_rep[:32, :], ALU.mult), vc)
    dbg("ppb", Ppb[:32, :], [32, 8])
    P.op("vector", lambda e: e.reduce_sum(Pme[:32, 0:1], cf8[:32, :], AX), vc)
    P.wait("vector", vc, vc.n)
    P.op("vector", lambda e: e.tensor_scalar(
        cf8[:32, :], Ppb[:32, :], Pme[:32, 0:1], None, ALU.subtract), vc)
    P.wait("vector", vc, vc.n)
    P.op("vector", lambda e: e.tensor_scalar(
        cf8[:32, :], cf8[:32, :], 0.0, None, ALU.max), vc)
    cfv = vc.n
    P.wait("scalar", vc, cfv)
    P.op("scalar", lambda e: e.activation(
        on8[:32, :], cf8[:32, :], AF.Exp, scale=-1.0), sc)
    cfs = sc.n
    P.wait("vector", sc, cfs)
    P.op("vector", lambda e: e.tensor_tensor(
        cf8[:32, :], on8[:32, :], incm_rep[:32, :], ALU.mult), vc)
    cf_done = vc.n
    dbg("cf8t", cf8[:32, :], [32, 8])
    P.wait("tensor", vc, cf_done)
    P.op("tensor", lambda e: e.transpose(
        psum[:8, 1280:1312], cf8[:32, :], idf[0:32, 0:32]), tc)
    cft_tr = tc.n
    P.wait("vector", tc, cft_tr)
    P.op("vector", lambda e: e.tensor_copy(cft[0:8, :],
                                           psum[:8, 1280:1312]), vc)
    P.wait("scalar", vc, vc.n)
    dma("scalar", coef_h[0, :].rearrange("(r h) -> r h", h=NH),
        cft[0:8, :])
    cfst = qc.n
    P.wait("gpsimd", qc, cfst)
    cfb = dma("gpsimd", coefrep[:, :], coef_h[0, :].partition_broadcast(128))
    P.op("vector", lambda e: e.memset(St[:], 0.0), vc)
    P.wait("vector", qg, cfb)
    for r in range(NCORES):
        ldr = dma("sync", Srt[:], slocal_out[r * DS:(r + 1) * DS, 0:DI])
        P.wait("vector", qs, ldr)
        rep_expand(Yg[:], coefrep[:, 32 * r:32 * (r + 1)], HD, NH)
        P.op("vector", lambda e: e.tensor_tensor(
            zldf[:], Srt[:], Yg[:], ALU.mult), vc)
        P.wait("vector", vc, vc.n)
        P.op("vector", lambda e: e.tensor_tensor(
            St[:], St[:], zldf[:], ALU.add), vc)
        P.wait("sync", vc, vc.n)
    sin_done = vc.n
    dbg("sin", St[:], [128, DI])

    for j in range(NCH):
        if True:
            P.wait("sync", vc, vc.n)
            ld1 = dma("sync", XsT[:], xsT_h[j * 128:(j + 1) * 128, :])
            ld2 = dma("sync", Xd[:], xd_h[j * 128:(j + 1) * 128, :])
            ld4 = dma("sync", zchunk[:], z_h[j * 128:(j + 1) * 128, :])
            P.wait("scalar", vc, vc.n)
            ld3 = dma("scalar", dS_sb[:], ds_h[j * 128:(j + 1) * 128, :])
            P.op("vector", lambda e: e.tensor_copy(Sbf[:], St[:]), vc)
            P.wait("vector", qc, ld3)
            stv = St[:].rearrange("p (i r) -> p r i", i=NH)
            lmv = lamrep[:, 32 * j:32 * (j + 1)].unsqueeze(1) \
                .broadcast_to([128, HD, NH])
            P.op("vector", lambda e, o_=stv, b_=lmv: e.tensor_tensor(
                o_, o_, b_, ALU.mult), vc)
            P.wait("vector", vc, vc.n)
            P.op("vector", lambda e: e.tensor_tensor(
                St[:], St[:], dS_sb[:], ALU.add), vc)
            P.wait("vector", vc, vc.n)
            rec_done = vc.n
            MT = MTa
            for q4 in range(4):
                P.wait("gpsimd", vc, vc.n)
                bq = dma("gpsimd", BCTq[:],
                         clb_h[j, 1024 * q4:1024 * (q4 + 1)]
                         .partition_broadcast(128))
                P.wait("vector", qg, bq)
                clv = clT_all[:, 32 * j:32 * (j + 1)].unsqueeze(1) \
                    .broadcast_to([128, 32, NH])
                P.op("vector", lambda e, v_=clv: e.tensor_tensor(
                    Dq[:].rearrange("p (t h) -> p t h", h=NH),
                    BCTq[:].rearrange("p (t h) -> p t h", h=NH),
                    v_, ALU.add), vc)
                P.op("vector", lambda e: e.tensor_scalar(
                    Dq2[:], Dq[:], 0.0, None, ALU.max), vc)
                dqv = vc.n
                P.wait("scalar", vc, dqv)
                P.op("scalar", lambda e: e.activation(
                    Dq[:], Dq2[:], AF.Exp, scale=-1.0), sc)
                dqs = sc.n
                P.wait("vector", sc, dqs)
                gtv = GTm[:, 128 * j + 32 * q4:128 * j + 32 * (q4 + 1)] \
                    .unsqueeze(1).broadcast_to([128, NH, 32])
                P.op("vector", lambda e, q_=q4, g_=gtv: e.tensor_tensor(
                    MTa[:].rearrange("p (h t) -> p h t", t=128)[
                        :, :, 32 * q_:32 * (q_ + 1)],
                    Dq[:].rearrange("p (t h) -> p t h", h=NH)
                    .transpose([0, 2, 1]),
                    g_, ALU.mult), vc)
            P.wait("vector", vc, vc.n)
            P.op("vector", lambda e: e.engine_nop(), vc)
            mt_done = vc.n
            P.wait("tensor", vc, mt_done)
            P.wait("tensor", qs, max(ld1, ld2))
            for h in range(NH):
                P.op("tensor", lambda e, h_=h: e.matmul(
                    psum[:, HD * h_:HD * (h_ + 1)],
                    MT[:, 128 * h_:128 * (h_ + 1)],
                    Xd[:, HD * h_:HD * (h_ + 1)],
                    start=True, stop=True), None)
            for half in range(2):
                for g2 in range(2):
                    P.op("tensor", lambda e, j_=j, h_=half, g_=g2: e.matmul(
                        bank(4 + 2 * h_ + g_, 512),
                        C_fm[:, 128 * j_:128 * (j_ + 1)],
                        Sbf[:, 1024 * h_ + 512 * g_:
                            1024 * h_ + 512 * (g_ + 1)],
                        start=True, stop=True),
                        tc if (half == 1 and g2 == 1) else None)
            ymm = tc.n
            P.wait("vector", tc, ymm)
            P.wait("vector", qs, ld4)
            for half in range(2):
                rep_expand(Ew[:], ECL_all[:, 32 * j + 16 * half:
                                          32 * j + 16 * (half + 1)], HD, 16)
                P.op("vector", lambda e, h_=half: e.tensor_tensor(
                    W1[:], bank(4 + 2 * h_, 1024), Ew[:], ALU.mult), vc)
                P.op("vector", lambda e, h_=half: e.tensor_tensor(
                    W2[:], W1[:], bank(0 + 2 * h_, 1024), ALU.add), vc)
                P.op("vector", lambda e, h_=half: e.tensor_tensor(
                    W1[:], XsT[:, 1024 * h_:1024 * (h_ + 1)],
                    drepb[:, 1024 * h_:1024 * (h_ + 1)], ALU.mult), vc)
                P.op("vector", lambda e: e.tensor_tensor(
                    Ew[:], W2[:], W1[:], ALU.add), vc)
                P.op("vector", lambda e, h_=half: e.tensor_tensor(
                    Yg[:, 1024 * h_:1024 * (h_ + 1)], Ew[:],
                    zchunk[:, 1024 * h_:1024 * (h_ + 1)], ALU.mult), vc)
            yg_done = vc.n
            P.wait("scalar", vc, yg_done)
            P.op("scalar", lambda e: e.activation(
                m_res[:], Yg[:, 0:1024], AF.Square,
                accum_out=rmsc[:, 0:1]), sc)
            P.op("scalar", lambda e: e.activation(
                m_res[:], Yg[:, 1024:2048], AF.Square,
                accum_out=rmsc[:, 1:2]), sc)
            sqd = sc.n
            P.wait("vector", sc, sqd)
            P.op("vector", lambda e: e.tensor_tensor(
                rmsc[:, 2:3], rmsc[:, 0:1], rmsc[:, 1:2], ALU.add), vc)
            P.wait("vector", vc, vc.n)
            P.op("vector", lambda e: e.tensor_scalar(
                rmsc[:, 3:4], rmsc[:, 2:3], 1.0 / DI, None, ALU.mult), vc)
            rv = vc.n
            P.wait("scalar", vc, rv)
            P.op("scalar", lambda e: e.activation(
                rmsc[:, 2:3], rmsc[:, 3:4], AF.Ln, bias=EPS), sc)
            P.op("scalar", lambda e: e.activation(
                rmsc[:, 3:4], rmsc[:, 2:3], AF.Exp, scale=-0.5), sc)
            rstd_done = sc.n
            P.wait("vector", sc, rstd_done)
            P.op("vector", lambda e: e.tensor_scalar(
                Ygn[:], Yg[:], rmsc[:, 3:4], None, ALU.mult), vc)
            ygn_done = vc.n
            if j == 0:
                dbg("ygn0", Ygn[:], [128, DI], BF16)
            if j == 0:
                dbg("yg0", Yg[:], [128, DI])
            P.wait("tensor", vc, ygn_done)
            evs = []
            for f in range(16):
                o_ = 512 * (f % 4)
                if f >= 4:
                    P.wait("tensor", sc, evs[f - 4])
                P.op("tensor", lambda e, f_=f, oo=o_: e.transpose(
                    psum[:, oo:oo + 64].bitcast(BF16),
                    Ygn[:, f_ * 128:(f_ + 1) * 128], idb[:]), tc)
                P.wait("scalar", tc, tc.n)
                P.op("scalar", lambda e, f_=f, oo=o_: e.activation(
                    YgT[:, f_ * 128:(f_ + 1) * 128],
                    psum[:, oo:oo + 64].bitcast(BF16), AF.Copy), sc)
                evs.append(sc.n)
            P.wait("tensor", sc, sc.n)
            for k in range(16):
                for jo in range(2):
                    P.op("tensor", lambda e, k_=k, o_=jo: e.matmul(
                        bank(4 + o_, 512), YgT[:, k_ * 128:(k_ + 1) * 128],
                        wout[:, k_ * D + 512 * o_:k_ * D + 512 * (o_ + 1)],
                        start=(k_ == 0), stop=(k_ == 15)),
                        tc if (k == 15 and jo == 1) else None)
            opd = tc.n
            P.wait("scalar", vc, vc.n)
            ldx1 = dma("scalar", x1t[:],
                       x1s_d[(j + 1) * 128:(j + 2) * 128, :])
            P.wait("vector", qc, ldx1)
            P.op("vector", lambda e, j_=j: e.tensor_scalar(
                x2t[:], x1t[:], stat_a1[:, j_ + 1:j_ + 2],
                stat_b1[:, j_ + 1:j_ + 2], ALU.mult, ALU.add), vc)
            if j == 0:
                dbg("x2a", x2t[:], [128, D])
            P.wait("vector", tc, opd)
            if j == 0:
                P.op("vector", lambda e: e.tensor_copy(
                    m_res[:], psum[:, 2048:3072]), vc)
                P.wait("vector", vc, vc.n)
                dbg("xproj", m_res[:], [128, D])
            P.op("vector", lambda e: e.tensor_tensor(
                x1t[:], x2t[:], psum[:, 2048:3072], ALU.add), vc)
            x2_done = vc.n
            if j == 0:
                dbg("x2t0", x1t[:], [128, D])
            P.wait("scalar", vc, x2_done)
            dma("scalar", x2s_d[j * 128:(j + 1) * 128, :], x1t[:])
            P.op("vector", lambda e, j_=j: e.reduce_sum(
                stat_m3[:, j_:j_ + 1], x1t[:], AX), vc)
            P.op("scalar", lambda e, j_=j: e.activation(
                m_res[:], x1t[:], AF.Square,
                accum_out=stat_sq[:, j_:j_ + 1]), sc)
            P.wait("vector", sc, sc.n)
            P.wait("tensor", vc, vc.n)
    ln3_sc = emit_stats_tail(stat_m3, stat_r3, 16)

    barrier()
    wq_loaded2 = load_ffn_weights()
    emit_ffn(x2s_d, x2s_d, out_d, stat_m3, stat_r3, ln3_sc, 16, wq_loaded2,
             name="2")
    barrier()

    with nc.Block() as block:
        P.run(block)
    ctx.close()
    mybir.codegen_inst_isa_subclasses(nc)
    return nc, dbg_names


def kernel(**inputs):
    global DBG_OUT
    per = _np_slice_inputs(inputs)
    nc, dbg_names = build_program(tuple(DBG_TAPS))
    res = run_bass_kernel_spmd(nc, per, core_ids=list(range(NCORES)))
    DBG_OUT = {name: [np.asarray(res.results[c][key]) for c in range(NCORES)]
               for name, key in dbg_names.items()}
    outs = [np.asarray(res.results[c]["out"], np.float32)
            for c in range(NCORES)]
    return np.concatenate(outs, axis=0).reshape(2, 8192, D)
